# version 01
# baseline (speedup 1.0000x reference)
import math
import numpy as np
import ml_dtypes
import concourse.bass as bass
import concourse.mybir as mybir
from concourse.bass_utils import run_bass_kernel_spmd

F32 = mybir.dt.float32
BF16 = mybir.dt.bfloat16
F16 = mybir.dt.float16
I32 = mybir.dt.int32
U16 = mybir.dt.uint16
AF = mybir.ActivationFunctionType
ALU = mybir.AluOpType
AX = mybir.AxisListType
BF = ml_dtypes.bfloat16

N_CORES = 8


STREAMS = {
    "pe": ("tensor", False),
    "act": ("scalar", False),
    "dve": ("vector", False),
    "pool": ("gpsimd", False),
    "sp": ("sync", True),
    "actq": ("scalar", True),
    "poolq": ("gpsimd", True),
}
SECTION_STREAMS = {"tensor": ["pe"], "scalar": ["act", "actq"],
                   "vector": ["dve"], "gpsimd": ["pool", "poolq"], "sync": ["sp"]}


class Op:
    __slots__ = ("idx", "stream", "emit", "deps", "inc", "cum", "cc_sem", "tag")

    def __init__(self, idx, stream, emit, tag):
        self.idx = idx
        self.stream = stream
        self.emit = emit
        self.deps = set()
        self.inc = 0
        self.cum = 0
        self.cc_sem = None
        self.tag = tag


class Prog:
    def __init__(self):
        self.ops = []
        self.last_write = {}
        self.reads_since = {}
        self.n_cc = 0

    def op(self, stream, emit, reads=(), writes=(), extra_deps=(), tag="", collective=False):
        assert stream in STREAMS
        o = Op(len(self.ops), stream, emit, tag)
        for k in reads:
            if k in self.last_write:
                o.deps.add(self.last_write[k])
        for k in writes:
            if k in self.last_write:
                o.deps.add(self.last_write[k])
            for r in self.reads_since.get(k, ()):
                o.deps.add(r)
        for d in extra_deps:
            if d is not None:
                o.deps.add(d)
        o.deps.discard(o.idx)
        self.ops.append(o)
        for k in reads:
            self.reads_since.setdefault(k, []).append(o.idx)
        for k in writes:
            self.last_write[k] = o.idx
            self.reads_since[k] = []
        if collective:
            o.cc_sem = self.n_cc
            self.n_cc += 1
        return o.idx

    def snapshot(self, prefix):
        out = set()
        for k, v in self.last_write.items():
            if isinstance(k, tuple) and k[:len(prefix)] == prefix:
                out.add(v)
        for k, rs in self.reads_since.items():
            if isinstance(k, tuple) and k[:len(prefix)] == prefix:
                out.update(rs)
        return out

    def finalize(self, nc, es):
        KSLOT = 8
        ops = self.ops
        for o in ops:
            o.deps = {dd for dd in o.deps
                      if not (ops[dd].stream == o.stream and o.stream == "pe")}
        needed = set()
        for o in ops:
            for d in o.deps:
                needed.add(d)
        cum = {s: 0 for s in STREAMS}
        dma_idx = {s: 0 for s in STREAMS}
        for o in ops:
            is_dma = STREAMS[o.stream][1]
            if o.cc_sem is not None:
                o.inc = 0
            elif is_dma:
                o.inc = 16
                j = dma_idx[o.stream]
                dma_idx[o.stream] += 1
                o.cum = (o.stream, j % KSLOT, 16 * (j // KSLOT + 1), j)
            elif o.idx in needed:
                o.inc = 1
                cum[o.stream] += 1
                o.cum = cum[o.stream]
            else:
                o.inc = 0
                o.cum = cum[o.stream]
        totals = dict(cum)
        dma_totals = dict(dma_idx)

        sems = {s: es.enter_context(nc.semaphore(f"sem_{s}"))
                for s in STREAMS if not STREAMS[s][1]}
        slot_sems = {s: [es.enter_context(nc.semaphore(f"sem_{s}{k}")) for k in range(KSLOT)]
                     for s in STREAMS if STREAMS[s][1]}
        cc_sems = [es.enter_context(nc.semaphore(f"sem_cc{i}")) for i in range(self.n_cc)]

        per_section = {sec: [] for sec in SECTION_STREAMS}
        for o in ops:
            per_section[STREAMS[o.stream][0]].append(o)

        block = es.enter_context(nc.Block())

        def make_section(sec, sec_ops):
            def body(eng):
                watermark = {s: 0 for s in STREAMS if not STREAMS[s][1]}
                wslot = {(s, k): 0 for s in STREAMS if STREAMS[s][1] for k in range(KSLOT)}
                watermark_cc = [0] * self.n_cc
                for o in sec_ops:
                    for d in sorted(o.deps):
                        p = ops[d]
                        if p.cc_sem is not None:
                            if watermark_cc[p.cc_sem] < 1:
                                eng.wait_ge(cc_sems[p.cc_sem], 1)
                                watermark_cc[p.cc_sem] = 1
                        elif STREAMS[p.stream][1]:
                            st_, sl_, val_, _j = p.cum
                            if val_ > wslot[(st_, sl_)]:
                                eng.wait_ge(slot_sems[st_][sl_], val_)
                                wslot[(st_, sl_)] = val_
                        else:
                            assert p.inc > 0, (o.tag, p.tag)
                            if p.cum > watermark[p.stream]:
                                eng.wait_ge(sems[p.stream], p.cum)
                                watermark[p.stream] = p.cum
                    if STREAMS[o.stream][1]:
                        st_, sl_, val_, j = o.cum
                        if j >= KSLOT and val_ - 16 > wslot[(st_, sl_)]:
                            eng.wait_ge(slot_sems[st_][sl_], val_ - 16)
                            wslot[(st_, sl_)] = val_ - 16
                        inst = o.emit(eng)
                        inst.then_inc(slot_sems[st_][sl_], 16)
                    else:
                        inst = o.emit(eng)
                        if o.cc_sem is not None:
                            inst.then_inc(cc_sems[o.cc_sem])
                        elif o.inc:
                            inst.then_inc(sems[o.stream], o.inc)
                if sec == "sync":
                    for s, t in totals.items():
                        if t > 0:
                            eng.wait_ge(sems[s], t)
                    for i in range(self.n_cc):
                        eng.wait_ge(cc_sems[i], 1)
                    for s, n in dma_totals.items():
                        for k in range(KSLOT):
                            cnt = (n - k + KSLOT - 1) // KSLOT if n > k else 0
                            if cnt > 0:
                                eng.wait_ge(slot_sems[s][k], 16 * cnt)
            return body

        for sec in ["scalar", "vector", "tensor", "gpsimd", "sync"]:
            getattr(block, sec)(make_section(sec, per_section[sec]))
        return block



class Cfg:
    def __init__(self, nblk=32, n_cores=8):
        self.NBLK = nblk
        self.NJ = nblk // 2
        self.TOK = self.NJ * 128
        self.D = 2048
        self.ECH = 16
        self.H = 16
        self.I = 5632
        self.IB_GU = 88
        self.IB = 44
        self.NE = 8
        self.GU_SH = 1408
        self.DN_SH = 768
        self.LN_EPS = 1e-5
        self.RMS_EPS = 1e-6
        self.SHIFT0 = 20.0
        self.n_cores = n_cores
        self.TC = 512
        self.NTC = self.TOK // self.TC if self.TOK >= self.TC else 1
        if self.TOK < self.TC:
            self.TC = self.TOK



def host_shard(cfg, inp):
    c = cfg
    x = np.asarray(inp["L0_attn_x"], np.float32)
    w_attn = np.asarray(inp["L0_attn_w_attn"], np.float32)
    b_attn = np.asarray(inp["L0_attn_b_attn"], np.float32)
    w_proj = np.asarray(inp["L0_attn_w_proj"], np.float32)
    b_proj = np.asarray(inp["L0_attn_b_proj"], np.float32)
    gup = np.asarray(inp["L0_ffn_gate_up_packed"], np.int32)
    gus = np.asarray(inp["L0_ffn_gate_up_scales"], ml_dtypes.float16 if inp["L0_ffn_gate_up_scales"].dtype != np.float16 else np.float16)
    dnp = np.asarray(inp["L0_ffn_down_packed"], np.int32)
    dns = np.asarray(inp["L0_ffn_down_scales"])
    ln_w = np.asarray(inp["L1_attn_ln_w"], np.float32)
    ln_b = np.asarray(inp["L1_attn_ln_b"], np.float32)
    w_in = np.asarray(inp["L1_attn_W_in"], np.float32)
    w_out = np.asarray(inp["L1_attn_W_out"], np.float32)
    gate_w = np.asarray(inp["L1_ffn_gate_w"], np.float32)
    gate_b = np.asarray(inp["L1_ffn_gate_b"], np.float32)
    exp_w = np.asarray(inp["L1_ffn_expert_w"], np.float32)
    exp_b = np.asarray(inp["L1_ffn_expert_b"], np.float32)

    D, H = c.D, c.H

    wqk = w_attn[:, :2 * D]
    ncb = 2 * D // 128
    wqk_lt = np.ascontiguousarray(
        wqk.reshape(c.ECH, 128, ncb, 128).transpose(2, 1, 0, 3).reshape(ncb * 128, D))
    wv_mv = np.ascontiguousarray(w_attn[:, 2 * D:])
    gus_t = np.ascontiguousarray(gus.T)
    dns_t = np.ascontiguousarray(np.asarray(dns).T)
    dns_t_pad = np.zeros((D, 48), dtype=dns_t.dtype)
    dns_t_pad[:, :44] = dns_t
    dnp_pad = np.zeros((D, 96 * 8), np.int32)
    dnp_pad[:, :704] = dnp

    ones_bf = np.ones((128, 128), BF)
    ones_f32 = np.ones((1, 128), np.float32)
    ident_bf = np.eye(128, dtype=BF)
    tri = np.where(np.tril(np.ones((128, 128), bool)).T, 0.0, -1e5).astype(np.float32)
    neg = np.full((128, 128), -1e5, np.float32)
    zero = np.zeros((128, 128), np.float32)
    mask_l0 = np.concatenate([tri, neg], 1)
    mask_l1 = np.concatenate([zero, tri], 1)
    battn_t = np.ascontiguousarray(b_attn[:2 * D].reshape(32, 128).T)
    cvec = np.zeros((128, 8), np.float32)
    cvec[:, 0] = c.LN_EPS
    cvec[:, 1] = c.RMS_EPS
    cvec[:, 2] = -c.SHIFT0
    cvec[:, 3] = 0.0
    gw_lt = np.ascontiguousarray(
        gate_w.reshape(c.ECH, 128, 8).transpose(1, 0, 2).reshape(128, c.ECH * 8))

    rot = 64
    inv_freq = 1.0 / (10000.0 ** (np.arange(0, rot, 2, dtype=np.float64) / rot))

    in_maps = []
    for core in range(cfg.n_cores):
        p, l = core // 2, core % 2
        own_blocks = np.arange(c.NJ) * 2 + l
        pos = (own_blocks[:, None] * 128 + np.arange(128)[None, :]).reshape(-1)
        xs = x[p].reshape(c.NBLK, 128, D)[own_blocks].reshape(c.TOK, D)
        ang = pos[:, None].astype(np.float64) * inv_freq[None, :]
        cos_own = np.cos(ang).astype(np.float32)
        sin_own = np.sin(ang).astype(np.float32)
        m = {
            "x0": np.ascontiguousarray(xs),
            "wqk_lt": wqk_lt, "wv_mv": wv_mv,
            "b_attn": b_attn.reshape(1, -1), "battn_t": battn_t,
            "w_proj": w_proj, "b_proj": b_proj.reshape(1, -1),
            "gup_pk": np.ascontiguousarray(gup[c.GU_SH * core: c.GU_SH * (core + 1)]),
            "gus_t": np.ascontiguousarray(gus_t[c.GU_SH * core: c.GU_SH * (core + 1)]),
            "dnp_pk": np.ascontiguousarray(dnp_pad[:, 96 * core: 96 * (core + 1)]),
            "dns_t": np.ascontiguousarray(dns_t_pad[:, 6 * core: 6 * (core + 1)]),
            "ln_w": ln_w.reshape(1, -1), "ln_b": ln_b.reshape(1, -1),
            "w_in": w_in, "w_out": w_out,
            "gw_lt": gw_lt, "gate_b": np.ascontiguousarray(gate_b.reshape(-1, 1)),
            "exp_w_sh": np.ascontiguousarray(exp_w[core]),
            "exp_b": np.ascontiguousarray(exp_b),
            "ones_bf": ones_bf, "ones_f32": ones_f32,
            "ident_bf": ident_bf, "ident_f32": np.eye(8, dtype=np.float32),
            "tri": mask_l0 if l == 0 else mask_l1,
            "cos_own": cos_own, "sin_own": sin_own,
            "cvec": cvec,
        }
        in_maps.append(m)
    return in_maps


def host_assemble(cfg, outs):
    c = cfg
    full = np.zeros((4, c.NBLK * 128, c.D), np.float32)
    for core in range(cfg.n_cores):
        p, l = core // 2, core % 2
        own_blocks = np.arange(c.NJ) * 2 + l
        full[p].reshape(c.NBLK, 128, c.D)[own_blocks] = outs[core].reshape(c.NJ, 128, c.D)
    return full



import contextlib


class Ctx:
    pass


def declare(cfg, nc, es):
    c = cfg
    x = Ctx()
    x.cfg, x.nc, x.es = c, nc, es
    x.prog = Prog()
    W = c.ECH * c.TOK

    def dram(name, shape, dt, kind="Internal"):
        return nc.dram_tensor(name, shape, dt, kind=kind).ap()

    d = x.d = {}
    d["x0"] = dram("x0", [c.TOK, c.D], F32, "ExternalInput")
    d["wqk_lt"] = dram("wqk_lt", [32 * 128, c.D], F32, "ExternalInput")
    d["wv_mv"] = dram("wv_mv", [c.D, c.D], F32, "ExternalInput")
    d["b_attn"] = dram("b_attn", [1, 3 * c.D], F32, "ExternalInput")
    d["battn_t"] = dram("battn_t", [128, 32], F32, "ExternalInput")
    d["w_proj"] = dram("w_proj", [c.D, c.D], F32, "ExternalInput")
    d["b_proj"] = dram("b_proj", [1, c.D], F32, "ExternalInput")
    d["gup_pk"] = dram("gup_pk", [c.GU_SH, 256], I32, "ExternalInput")
    d["gus_t"] = dram("gus_t", [c.GU_SH, 16], F16, "ExternalInput")
    d["dnp_pk"] = dram("dnp_pk", [c.D, 96], I32, "ExternalInput")
    d["dns_t"] = dram("dns_t", [c.D, 6], F16, "ExternalInput")
    d["ln_w"] = dram("ln_w", [1, c.D], F32, "ExternalInput")
    d["ln_b"] = dram("ln_b", [1, c.D], F32, "ExternalInput")
    d["w_in"] = dram("w_in", [c.D, 3 * c.D], F32, "ExternalInput")
    d["w_out"] = dram("w_out", [c.D, c.D], F32, "ExternalInput")
    d["gw_lt"] = dram("gw_lt", [128, c.ECH * 8], F32, "ExternalInput")
    d["gate_b"] = dram("gate_b", [8, 1], F32, "ExternalInput")
    d["exp_w_sh"] = dram("exp_w_sh", [c.D, c.D], F32, "ExternalInput")
    d["exp_b"] = dram("exp_b", [8, c.D], F32, "ExternalInput")
    d["ones_bf"] = dram("ones_bf", [128, 128], BF16, "ExternalInput")
    d["ones_f32"] = dram("ones_f32", [1, 128], F32, "ExternalInput")
    d["ident_bf"] = dram("ident_bf", [128, 128], BF16, "ExternalInput")
    d["tri"] = dram("tri", [128, 256], F32, "ExternalInput")
    d["ident_f32"] = dram("ident_f32", [8, 8], F32, "ExternalInput")
    d["cvec"] = dram("cvec", [128, 8], F32, "ExternalInput")
    d["cos_own"] = dram("cos_own", [c.TOK, 32], F32, "ExternalInput")
    d["sin_own"] = dram("sin_own", [c.TOK, 32], F32, "ExternalInput")
    d["y"] = dram("y", [c.TOK, c.D], F32, "ExternalOutput")
    dk = "ExternalOutput" if getattr(c, "dbg_ext", False) else "Internal"
    d["x_mid"] = dram("x_mid", [c.TOK, c.D], F32, dk)
    d["xp"] = dram("xp", [c.TOK, c.D], F32, dk)
    d["xn_tm_d"] = dram("xn_tm_d", [c.TOK, c.D], F32, dk)
    d["q_dram"] = dram("q_dram", [128, W], BF16, dk)
    d["kv_in"] = dram("kv_in", [256, W], BF16)
    d["kv_out"] = dram("kv_out", [512, W], BF16)
    d["wgu_in"] = dram("wgu_in", [128, 11 * c.D], BF16)
    d["wgu"] = dram("wgu", [1024, 11 * c.D], BF16)
    d["wdn_in"] = dram("wdn_in", [128, 6 * c.D], BF16)
    d["wdn"] = dram("wdn", [1024, 6 * c.D], BF16)
    d["expw_in"] = dram("expw_in", [c.D, c.D], BF16)
    d["dbg_o"] = dram("dbg_o", [128, max(c.ECH * c.TOK, 16384)], BF16)
    d["expw"] = dram("expw", [8 * c.D, c.D], BF16)

    def sb(name, shape, dt):
        return es.enter_context(nc.sbuf_tensor(name, shape, dt))

    s = x.s = {}
    s["xnfm"] = sb("xnfm", [128, W], BF16)
    ar1 = sb("ar1", [128, max(W, 16384)], BF16)
    s["ar1"] = ar1
    s["ar1_f32"] = ar1.bitcast(F32)
    REG = c.NBLK * 128 // 2
    s["REG"] = REG
    s["dq_wdq"] = 0
    s["dq_pk"] = 2048
    s["dq_nib"] = 2560
    s["dq_sb"] = 3072
    s["dq_b16"] = 3584
    s["dq_sct"] = 3840
    s["dq_sc2"] = 3872
    s["dq_stg"] = 3936
    B0 = 4704
    s["gb_off"] = B0
    s["xnt_off"] = B0 + 4096
    s["kh_off"] = B0
    s["vh_off"] = B0 + 3 * REG
    s["qh_off"] = B0 + 5 * REG
    s["pt_off"] = B0 + 5 * REG + 1024
    s["rr_off"] = B0 + 5 * REG + 3072
    s["stg_off"] = B0 + 5 * REG + 4096
    AEND = B0 + max(5 * REG + 5120, 6144)
    s["brow_off"] = AEND
    SC1 = max(c.IB * c.TC, AEND + 1024)
    sc1 = sb("sc1", [128, SC1], BF16)
    s["sc1"] = sc1
    s["hsb"] = sc1
    s["sc1_f32"] = sc1.bitcast(F32)
    s["sc1_i32"] = sc1.bitcast(I32)
    s["sc1_u16"] = sc1.bitcast(U16)
    s["sc1_f16"] = sc1.bitcast(F16)
    s["fx"] = sb("fx", [128, 2 * c.D], F32)
    s["wbuf"] = sb("wbuf", [128, 8192], BF16)
    s["identf"] = sb("identf", [8, 8], F32)
    s["cst"] = sb("cst", [128, 8], F32)
    s["stat"] = sb("stat", [128, 16], F32)
    s["lgt"] = sb("lgt", [128, 8], F32)
    s["lgs"] = sb("lgs", [128, 8], F32)
    s["msk"] = sb("msk", [128, c.NJ * 8], F32)
    s["gateb"] = sb("gateb", [8, 1], F32)
    s["gwlt"] = sb("gwlt", [128, c.ECH * 8], BF16)
    s["onesb"] = sb("onesb", [128, 128], BF16)
    s["onesf"] = sb("onesf", [1, 128], F32)
    s["identb"] = sb("identb", [128, 128], BF16)
    s["trif"] = sb("trif", [128, 256], F32)
    s["battn"] = sb("battn", [128, 32], F32)
    s["cost"] = sb("cost", [128, 32], F32)
    s["sint"] = sb("sint", [128, 32], F32)

    s["brow"] = sc1[0:1, AEND:AEND + 1024]
    s["rsc"] = s["fx"][:, 512:1024]
    s["lgf"] = s["fx"][0:8, 1024:1536]
    x.psc = es.enter_context(nc.psum_tensor("psc", [128, 2048], F32))
    pbig = [x.psc[:, i * 512:(i + 1) * 512] for i in range(4)]
    rest = [es.enter_context(nc.psum_tensor(f"pb{i}", [128, 512], F32)) for i in range(4, 8)]
    x.pb = pbig + [t[:, :] for t in rest]
    x.pb_bf = [None] * 4 + [t.bitcast(BF16)[:, :] for t in rest]
    x.psc_ap = x.psc[:, :]
    return x



def ph_consts(x):
    c, p, d, s = x.cfg, x.prog, x.d, x.s
    for name, dst in [("ones_bf", s["onesb"]), ("ones_f32", s["onesf"]),
                      ("ident_bf", s["identb"]), ("tri", s["trif"]),
                      ("ident_f32", s["identf"]), ("cvec", s["cst"]),
                      ("battn_t", s["battn"]), ("gate_b", s["gateb"])]:
        p.op("sp", lambda e, nm=name, t=dst: e.dma_start(out=t[:, :], in_=d[nm]),
             reads=[(name,)], writes=[("sb", name)], tag=f"ld_{name}")
    p.op("poolq", lambda e: e.dma_start(out=s["gwlt"][:, :], in_=d["gw_lt"]),
         reads=["gw_lt"], writes=[("sb", "gwlt")], tag="ld_gwlt")


def ph_dequant(x):
    c, p, d, s = x.cfg, x.prog, x.d, x.s
    sc1 = s["sc1"]
    i32, u16, f16v, f32v = s["sc1_i32"], s["sc1_u16"], s["sc1_f16"], s["sc1_f32"]
    wdq = sc1[:, s["dq_wdq"]:s["dq_wdq"] + 2048]
    pk = i32[:, s["dq_pk"] // 2:s["dq_pk"] // 2 + 256]
    nib = i32[:, s["dq_nib"] // 2:s["dq_nib"] // 2 + 256]
    sbits = i32[:, s["dq_sb"] // 2:s["dq_sb"] // 2 + 256]
    b16 = u16[:, s["dq_b16"]:s["dq_b16"] + 256]
    sct = f16v[:, s["dq_sct"]:s["dq_sct"] + 16]
    sc2 = f32v[:, s["dq_sc2"] // 2:s["dq_sc2"] // 2 + 16]
    gustg = sc1[:, s["dq_stg"]:s["dq_stg"] + 512]
    dnstg = sc1[:, s["dq_stg"] + 512:s["dq_stg"] + 768]

    def dequant_tile(src_pk, src_sc, pcols, ngroups, tag):
        rep = 8 * pcols // ngroups
        p.op("sp", lambda e: e.dma_start(out=pk[:, :pcols], in_=src_pk),
             reads=[], writes=[("dq", "pk")], tag=f"{tag}_pk")
        p.op("sp", lambda e: e.dma_start(out=sct[:, :ngroups], in_=src_sc),
             reads=[], writes=[("dq", "sct")], tag=f"{tag}_sc")
        p.op("dve", lambda e: e.tensor_copy(out=sc2[:, :ngroups], in_=sct[:, :ngroups]),
             reads=[("dq", "sct")], writes=[("dq", "sc2")], tag=f"{tag}_cp")
        p.op("dve", lambda e: e.tensor_scalar(out=sc2[:, :ngroups], in0=sc2[:, :ngroups],
                                              scalar1=16384.0, scalar2=None, op0=ALU.mult),
             reads=[("dq", "sc2")], writes=[("dq", "sc2")], tag=f"{tag}_sm")
        for k in range(8):
            p.op("dve", lambda e, k=k: e.tensor_scalar(
                out=nib[:, :pcols], in0=pk[:, :pcols], scalar1=4 * k, scalar2=15,
                op0=ALU.logical_shift_right, op1=ALU.bitwise_and),
                reads=[("dq", "pk")], writes=[("dq", "nib")], tag=f"{tag}_n{k}")
            p.op("dve", lambda e: e.tensor_scalar(
                out=sbits[:, :pcols], in0=nib[:, :pcols], scalar1=8, scalar2=12,
                op0=ALU.bitwise_and, op1=ALU.logical_shift_left),
                reads=[("dq", "nib")], writes=[("dq", "sb")], tag=f"{tag}_s{k}")
            p.op("dve", lambda e: e.tensor_scalar(
                out=nib[:, :pcols], in0=nib[:, :pcols], scalar1=7, scalar2=9,
                op0=ALU.bitwise_and, op1=ALU.logical_shift_left),
                reads=[("dq", "nib")], writes=[("dq", "nib")], tag=f"{tag}_m{k}")
            p.op("dve", lambda e: e.tensor_tensor(
                out=sbits[:, :pcols], in0=sbits[:, :pcols], in1=nib[:, :pcols],
                op=ALU.bitwise_or),
                reads=[("dq", "nib"), ("dq", "sb")], writes=[("dq", "sb")], tag=f"{tag}_o{k}")
            p.op("dve", lambda e: e.tensor_copy(out=b16[:, :pcols], in_=sbits[:, :pcols]),
                 reads=[("dq", "sb")], writes=[("dq", "b16")], tag=f"{tag}_c{k}")
            p.op("dve", lambda e, k=k, pcols=pcols, ngroups=ngroups, rep=rep: e.tensor_tensor(
                out=wdq[:, k:8 * pcols:8].rearrange("p (g r) -> p g r", g=ngroups),
                in0=b16[:, :pcols].bitcast(F16).rearrange("p (g r) -> p g r", g=ngroups),
                in1=sc2[:, :ngroups].unsqueeze(2).broadcast_to([128, ngroups, rep // 8]),
                op=ALU.mult),
                reads=[("dq", "b16"), ("dq", "sc2")], writes=[("dq", "w")], tag=f"{tag}_w{k}")

    for lb in range(11):
        dequant_tile(d["gup_pk"][lb * 128:(lb + 1) * 128, :],
                     d["gus_t"][lb * 128:(lb + 1) * 128, :], 256, 16, f"gu{lb}")
        for fq in range(4):
            for j in range(4):
                fc = fq * 4 + j
                bank = 7
                p.op("pe", lambda e, fc=fc, bank=bank: e.transpose(
                    x.pb_bf[bank][:, 0:128], wdq[:, fc * 128:(fc + 1) * 128],
                    s["identb"][:, :]),
                    reads=[("dq", "w"), ("sb", "ident_bf")], writes=[("pb", bank)],
                    tag=f"gu{lb}t{fc}")
                p.op("act", lambda e, j=j, bank=bank: e.copy(
                    out=gustg[:, j * 128:(j + 1) * 128], in_=x.pb_bf[bank][:, 0:128]),
                    reads=[("pb", bank)], writes=[("dq", "gustg", j)], tag=f"gu{lb}e{fc}")
            p.op("sp", lambda e, lb=lb, fq=fq: e.dma_start(
                out=d["wgu_in"][:, lb * 2048 + fq * 512: lb * 2048 + (fq + 1) * 512],
                in_=gustg),
                reads=[("dq", "gustg", j2) for j2 in range(4)], writes=[("wgu_in",)],
                tag=f"gu{lb}s{fq}")
    for hc in range(c.ECH):
        dequant_tile(d["dnp_pk"][hc * 128:(hc + 1) * 128, :],
                     d["dns_t"][hc * 128:(hc + 1) * 128, :], 96, 6, f"dn{hc}")
        for lb in range(6):
            bank = 7
            sl = lb % 2
            p.op("pe", lambda e, lb=lb, bank=bank: e.transpose(
                x.pb_bf[bank][:, 0:128], wdq[:, lb * 128:(lb + 1) * 128], s["identb"][:, :]),
                reads=[("dq", "w"), ("sb", "ident_bf")], writes=[("pb", bank)],
                tag=f"dn{hc}t{lb}")
            p.op("act", lambda e, sl=sl, bank=bank: e.copy(
                out=dnstg[:, sl * 128:(sl + 1) * 128], in_=x.pb_bf[bank][:, 0:128]),
                reads=[("pb", bank)], writes=[("dq", "dnstg", sl)], tag=f"dn{hc}e{lb}")
            p.op("sp", lambda e, lb=lb, hc=hc, sl=sl: e.dma_start(
                out=d["wdn_in"][:, lb * 2048 + hc * 128: lb * 2048 + (hc + 1) * 128],
                in_=dnstg[:, sl * 128:(sl + 1) * 128]),
                reads=[("dq", "dnstg", sl)], writes=[("wdn_in",)], tag=f"dn{hc}s{lb}")
    p.op("pool", lambda e: e.collective_compute(
        "AllGather", ALU.bypass, replica_groups=[list(range(8))],
        ins=[d["wgu_in"].opt()], outs=[d["wgu"].opt()]),
        reads=[("wgu_in",)], writes=[("wgu",)], tag="ag_wgu", collective=True)
    p.op("pool", lambda e: e.collective_compute(
        "AllGather", ALU.bypass, replica_groups=[list(range(8))],
        ins=[d["wdn_in"].opt()], outs=[d["wdn"].opt()]),
        reads=[("wdn_in",)], writes=[("wdn",)], tag="ag_wdn", collective=True)


def ph_experts_convert(x):
    c, p, d, s = x.cfg, x.prog, x.d, x.s
    wb = s["wbuf"]
    for fc in range(c.ECH):
        b = fc % 2
        p.op("poolq", lambda e, fc=fc, b=b: e.dma_start(
            out=wb[:, b * 2048:(b + 1) * 2048], in_=d["exp_w_sh"][fc * 128:(fc + 1) * 128, :]),
            reads=[("exp_w_sh",)], writes=wbk(b * 2048, 2048), tag=f"xc{fc}")
        p.op("sp", lambda e, fc=fc, b=b: e.dma_start(
            out=d["expw_in"][fc * 128:(fc + 1) * 128, :], in_=wb[:, b * 2048:(b + 1) * 2048]),
            reads=wbk(b * 2048, 2048), writes=[("expw_in",)], tag=f"xs{fc}")
    p.op("pool", lambda e: e.collective_compute(
        "AllGather", ALU.bypass, replica_groups=[list(range(8))],
        ins=[d["expw_in"].opt()], outs=[d["expw"].opt()]),
        reads=[("expw_in",)], writes=[("expw",)], tag="ag_exp", collective=True)


def ph_ln(x, L, x_src):
    c, p, d, s = x.cfg, x.prog, x.d, x.s
    sc1 = s["sc1"]
    inv_d = 1.0 / c.D
    fence = (p.snapshot(("at",)) | p.snapshot(("h",)) | p.snapshot(("a1",))
             | p.snapshot(("fx",)) | p.snapshot(("lgf",)) | p.snapshot(("rp",)))
    for tb in range(c.NJ):
        sl = tb % 2
        xt = s["fx"][:, sl * c.D:(sl + 1) * c.D]
        p.op("sp", lambda e, tb=tb, xt=xt: e.dma_start(out=xt, in_=x_src[tb * 128:(tb + 1) * 128, :]),
             reads=[(x_src.tensor.name,)], writes=fxk(sl * c.D, 2048), extra_deps=fence,
             tag=f"lnx{tb}")
        st = s["stat"]
        p.op("dve", lambda e, xt=xt: e.reduce_sum(out=st[:, 0:1], in_=xt, axis=AX.X),
             reads=fxk(sl * c.D, 2048), writes=[("st", 0)], tag=f"lnmu{tb}")
        xnt = sc1[:, s["xnt_off"]:s["xnt_off"] + 2048]
        p.op("act", lambda e, xt=xt, xnt=xnt: e.activation(
            out=xnt, in_=xt, func=AF.Square, accum_out=st[:, 1:2]),
            reads=fxk(sl * c.D, 2048), writes=[("st", 1), ("xnt",)], extra_deps=fence,
            tag=f"lnsq{tb}")
        p.op("dve", lambda e: e.tensor_scalar(out=st[:, 2:3], in0=st[:, 0:1],
                                              scalar1=inv_d, scalar2=None, op0=ALU.mult),
             reads=[("st", 0)], writes=[("st", 2)], tag=f"lnm{tb}")
        p.op("dve", lambda e: e.tensor_scalar(out=st[:, 3:4], in0=st[:, 2:3],
                                              scalar1=st[:, 2:3], scalar2=None, op0=ALU.mult),
             reads=[("st", 2)], writes=[("st", 3)], tag=f"lnm2{tb}")
        p.op("dve", lambda e: e.scalar_tensor_tensor(out=st[:, 4:5], in0=st[:, 1:2],
                                                     scalar=inv_d, in1=st[:, 3:4],
                                                     op0=ALU.mult, op1=ALU.subtract),
             reads=[("st", 1), ("st", 3)], writes=[("st", 4)], tag=f"lnv{tb}")
        p.op("act", lambda e: e.activation(out=st[:, 5:6], in_=st[:, 4:5], func=AF.Sqrt,
                                           bias=s["cst"][:, 0:1], scale=1.0),
             reads=[("st", 4), ("sb", "cvec")], writes=[("st", 5)], tag=f"lnsd{tb}")
        p.op("dve", lambda e: e.reciprocal(out=st[:, 6:7], in_=st[:, 5:6]),
             reads=[("st", 5)], writes=[("st", 6)], tag=f"lnr{tb}")
        if L == 0:
            p.op("dve", lambda e, xt=xt, xnt=xnt: e.tensor_scalar(
                out=xnt, in0=xt, scalar1=st[:, 2:3], scalar2=st[:, 6:7],
                op0=ALU.subtract, op1=ALU.mult),
                reads=fxk(sl * c.D, 2048) + [("st", 2), ("st", 6)], writes=[("xnt",)],
                tag=f"lnn{tb}")
        else:
            tt_ = s["fx"][:, (1 - sl) * c.D:(2 - sl) * c.D]
            p.op("dve", lambda e, xt=xt, tt_=tt_: e.tensor_scalar(
                out=tt_, in0=xt, scalar1=st[:, 2:3], scalar2=st[:, 6:7],
                op0=ALU.subtract, op1=ALU.mult),
                reads=fxk(sl * c.D, 2048) + [("st", 2), ("st", 6)],
                writes=fxk((1 - sl) * c.D, 2048), tag=f"lnn{tb}")
            p.op("sp", lambda e, tb=tb, tt_=tt_: e.dma_start(
                out=d["xn_tm_d"][tb * 128:(tb + 1) * 128, :], in_=tt_),
                reads=fxk((1 - sl) * c.D, 2048), writes=[("xn_tm_d",)], tag=f"lnst{tb}")
            p.op("dve", lambda e, tt_=tt_: e.reduce_sum(out=st[:, 0:1], in_=tt_, axis=AX.X),
                 reads=fxk((1 - sl) * c.D, 2048), writes=[("st", 0)], tag=f"ln2mu{tb}")
            p.op("act", lambda e, tt_=tt_, xnt=xnt: e.activation(
                out=xnt, in_=tt_, func=AF.Square, accum_out=st[:, 1:2]),
                reads=fxk((1 - sl) * c.D, 2048), writes=[("st", 1), ("xnt",)], tag=f"ln2sq{tb}")
            p.op("dve", lambda e: e.tensor_scalar(out=st[:, 2:3], in0=st[:, 0:1],
                                                  scalar1=inv_d, scalar2=None, op0=ALU.mult),
                 reads=[("st", 0)], writes=[("st", 2)], tag=f"ln2m{tb}")
            p.op("dve", lambda e: e.tensor_scalar(out=st[:, 3:4], in0=st[:, 2:3],
                                                  scalar1=st[:, 2:3], scalar2=None, op0=ALU.mult),
                 reads=[("st", 2)], writes=[("st", 3)], tag=f"ln2m2{tb}")
            p.op("dve", lambda e: e.scalar_tensor_tensor(out=st[:, 4:5], in0=st[:, 1:2],
                                                         scalar=inv_d, in1=st[:, 3:4],
                                                         op0=ALU.mult, op1=ALU.subtract),
                 reads=[("st", 1), ("st", 3)], writes=[("st", 4)], tag=f"ln2v{tb}")
            p.op("act", lambda e: e.activation(out=st[:, 5:6], in_=st[:, 4:5], func=AF.Sqrt,
                                               bias=s["cst"][:, 0:1], scale=1.0),
                 reads=[("st", 4), ("sb", "cvec")], writes=[("st", 5)], tag=f"ln2sd{tb}")
            p.op("dve", lambda e: e.reciprocal(out=st[:, 6:7], in_=st[:, 5:6]),
                 reads=[("st", 5)], writes=[("st", 6)], tag=f"ln2r{tb}")
            p.op("dve", lambda e, tt_=tt_, xnt=xnt: e.tensor_scalar(
                out=xnt, in0=tt_, scalar1=st[:, 2:3], scalar2=st[:, 6:7],
                op0=ALU.subtract, op1=ALU.mult),
                reads=fxk((1 - sl) * c.D, 2048) + [("st", 2), ("st", 6)], writes=[("xnt",)],
                tag=f"ln2n{tb}")
            grep = sc1[:, s["gb_off"]:s["gb_off"] + 2048]
            brep = sc1[:, s["gb_off"] + 2048:s["gb_off"] + 4096]
            p.op("dve", lambda e, xnt=xnt, grep=grep: e.tensor_tensor(
                out=xnt, in0=xnt, in1=grep, op=ALU.mult),
                reads=[("xnt",), ("gbrep", 0)], writes=[("xnt",)], tag=f"lng{tb}")
            p.op("dve", lambda e, xnt=xnt, brep=brep: e.tensor_tensor(
                out=xnt, in0=xnt, in1=brep, op=ALU.add),
                reads=[("xnt",), ("gbrep", 1)], writes=[("xnt",)], tag=f"lnb{tb}")
        for fc in range(c.ECH):
            bank = 4 + (fc % 2)
            p.op("pe", lambda e, fc=fc, bank=bank, xnt=xnt: e.transpose(
                x.pb_bf[bank][:, 0:128], xnt[:, fc * 128:(fc + 1) * 128], s["identb"][:, :]),
                reads=[("xnt",), ("sb", "ident_bf")], writes=[("pb", bank)], tag=f"lt{tb}_{fc}")
            p.op("act", lambda e, fc=fc, tb=tb, bank=bank: e.copy(
                out=s["xnfm"][:, fc * c.TOK + tb * 128: fc * c.TOK + (tb + 1) * 128],
                in_=x.pb_bf[bank][:, 0:128]),
                reads=[("pb", bank)], writes=[("xnfm", fc, tb)], tag=f"le{tb}_{fc}")


def ph_gamma_beta(x):
    c, p, d, s = x.cfg, x.prog, x.d, x.s
    sc1 = s["sc1"]
    fence = (p.snapshot(("at",)) | p.snapshot(("h",)) | p.snapshot(("dq",))
             | p.snapshot(("fx",)))
    for i, nm in enumerate(["ln_w", "ln_b"]):
        dst = sc1[:, s["gb_off"] + i * 2048:s["gb_off"] + (i + 1) * 2048]
        for q in range(4):
            bank = 4 + (q % 2)
            p.op("sp", lambda e, nm=nm, q=q: e.dma_start(
                out=s["rsc"][0:1, 0:512], in_=d[nm][:, q * 512:(q + 1) * 512]),
                reads=[(nm,)], writes=[("fx", 1)], extra_deps=fence, tag=f"gb{nm}{q}")
            p.op("pe", lambda e, bank=bank: e.matmul(
                x.pb[bank][:, 0:512], s["onesf"][:, :], s["rsc"][0:1, 0:512],
                start=True, stop=True),
                reads=[("fx", 1), ("sb", "ones_f32")], writes=[("pb", bank)],
                extra_deps=fence, tag=f"gbm{nm}{q}")
            p.op("act", lambda e, dst=dst, q=q, bank=bank: e.copy(
                out=dst[:, q * 512:(q + 1) * 512], in_=x.pb[bank][:, 0:512]),
                reads=[("pb", bank)], writes=[("gbrep", i)], extra_deps=fence, tag=f"gbe{nm}{q}")


def ph_qkv0(x):
    c, p, d, s = x.cfg, x.prog, x.d, x.s
    sc1 = s["sc1"]
    NTC = max(1, c.TOK // 512)
    stg = lambda sl: sc1[:, s["stg_off"] + sl * 512: s["stg_off"] + (sl + 1) * 512]
    fence = p.snapshot(("a1",))
    for cb in range(32):
        b = cb % 2
        p.op("poolq", lambda e, cb=cb, b=b: e.dma_start(
            out=s["wbuf"][:, b * 2048:(b + 1) * 2048],
            in_=d["wqk_lt"][cb * 128:(cb + 1) * 128, :]),
            reads=[("wqk_lt",)], writes=[("wb", b)], tag=f"qkw{cb}")
        for tc in range(NTC):
            bank = (cb * NTC + tc) % 4
            for fc in range(c.ECH):
                p.op("pe", lambda e, cb=cb, tc=tc, fc=fc, b=b, bank=bank: e.matmul(
                    x.pb[bank][:, 0:512],
                    s["wbuf"][:, b * 2048 + fc * 128: b * 2048 + (fc + 1) * 128],
                    s["xnfm"][:, fc * c.TOK + tc * 512: fc * c.TOK + (tc + 1) * 512],
                    start=(fc == 0), stop=(fc == c.ECH - 1)),
                    reads=[("wb", b)] + [("xnfm", fc, 4 * tc + i) for i in range(4)],
                    writes=[("pb", bank)], tag=f"qk{cb}_{tc}_{fc}")
            sl = bank % 2
            p.op("act", lambda e, cb=cb, bank=bank, sl=sl: e.activation(
                out=stg(sl), in_=x.pb[bank][:, 0:512], func=AF.Identity,
                bias=s["battn"][:, cb:cb + 1], scale=1.0),
                reads=[("pb", bank), ("sb", "battn_t")], writes=[("at", "stg", sl)],
                tag=f"qke{cb}_{tc}")
            if cb < 16:
                dst = d["q_dram"][:, cb * c.TOK + tc * 512: cb * c.TOK + (tc + 1) * 512]
            else:
                dst = d["kv_in"][0:128, (cb - 16) * c.TOK + tc * 512: (cb - 16) * c.TOK + (tc + 1) * 512]
            wkey = ("q_dram",) if cb < 16 else ("kv_in",)
            p.op("sp", lambda e, dst=dst, sl=sl: e.dma_start(out=dst, in_=stg(sl)),
                 reads=[("at", "stg", sl)], writes=[wkey], tag=f"qks{cb}_{tc}")
    for cc in range(4):
        for fq in range(4):
            p.op("poolq", lambda e, cc=cc, fq=fq: e.dma_start(
                out=s["wbuf"][:, fq * 2048:(fq + 1) * 2048].rearrange("p (c n) -> p c n", n=512),
                in_=d["wv_mv"][fq * 512:(fq + 1) * 512, cc * 512:(cc + 1) * 512]
                .rearrange("(c p) n -> p c n", p=128)),
                reads=[("wv_mv",)], writes=[("wb", fq)], tag=f"vw{cc}_{fq}")
        p.op("poolq", lambda e, cc=cc: e.dma_start(
            out=s["brow"][0:1, 0:512], in_=d["b_attn"][:, 2 * c.D + cc * 512: 2 * c.D + (cc + 1) * 512]),
            reads=[("b_attn",)], writes=[("brow", 0)], tag=f"vb{cc}")
        for tb in range(c.NJ):
            bank = tb % 4
            for fc in range(c.ECH):
                p.op("pe", lambda e, cc=cc, tb=tb, fc=fc, bank=bank: e.matmul(
                    x.pb[bank][:, 0:512],
                    s["xnfm"][:, fc * c.TOK + tb * 128: fc * c.TOK + (tb + 1) * 128],
                    s["wbuf"][:, (fc // 4) * 2048 + (fc % 4) * 512: (fc // 4) * 2048 + (fc % 4 + 1) * 512],
                    start=(fc == 0), stop=False),
                    reads=[("wb", fc // 4), ("xnfm", fc, tb)],
                    writes=[("pb", bank)], tag=f"v{cc}_{tb}_{fc}")
            p.op("pe", lambda e, bank=bank: e.matmul(
                x.pb[bank][:, 0:512], s["onesb"][0:1, 0:128], s["brow"][0:1, 0:512],
                start=False, stop=True),
                reads=[("brow", 0), ("sb", "ones_bf")], writes=[("pb", bank)],
                tag=f"vbias{cc}_{tb}")
            p.op("act", lambda e, cc=cc, tb=tb, bank=bank: e.copy(
                out=s["ar1"][:, 0:c.ECH * c.TOK]
                .rearrange("p (h j n) -> p h j n", h=c.H, j=c.NJ)
                [:, 4 * cc:4 * cc + 4, tb:tb + 1, :],
                in_=x.pb[bank][:, 0:512].rearrange("p (j o n) -> p j o n", j=4, o=1)),
                reads=[("pb", bank)], writes=[("a1", "v", cc, tb)], extra_deps=fence,
                tag=f"ve{cc}_{tb}")


def ph_ag_kv(x, L):
    c, p, d, s = x.cfg, x.prog, x.d, x.s
    Wc = c.ECH * c.TOK
    vreads = [("a1", "v", cc, tb) for cc in range(4) for tb in range(c.NJ)]
    for q in range(4):
        p.op("sp", lambda e, q=q: e.dma_start(
            out=d["kv_in"][128:256, q * (Wc // 4):(q + 1) * (Wc // 4)],
            in_=s["ar1"][:, q * (Wc // 4):(q + 1) * (Wc // 4)]),
            reads=vreads, writes=[("kv_in",)], tag=f"vst{q}")
    p.op("pool", lambda e: e.collective_compute(
        "AllGather", ALU.bypass,
        replica_groups=[[0, 1], [2, 3], [4, 5], [6, 7]],
        ins=[d["kv_in"].opt()], outs=[d["kv_out"].opt()]),
        reads=[("kv_in",)], writes=[("kv_out",)], tag=f"ag_kv{L}", collective=True)


def ph_attn(x, L):
    c, p, d, s = x.cfg, x.prog, x.d, x.s
    sc1 = s["sc1"]
    NQC = max(1, c.NJ // 4)
    REG = s["REG"]
    scale = (1.0 / math.sqrt(128.0)) if L == 1 else 1.0
    shift = -c.SHIFT0 if L == 0 else 0.0
    o_scale = (1.0 / math.sqrt(c.D)) if L == 0 else 1.0
    causal = (L == 0)
    fence = p.snapshot(("a1",)) | p.snapshot(("xnt",)) | p.snapshot(("gbrep",)) | p.snapshot(("gbrow",))
    kh = lambda sl: sc1[:, s["kh_off"] + sl * REG: s["kh_off"] + (sl + 1) * REG]
    vh = lambda sl: sc1[:, s["vh_off"] + sl * REG: s["vh_off"] + (sl + 1) * REG]
    qh = lambda sl: sc1[:, s["qh_off"] + sl * 512: s["qh_off"] + (sl + 1) * 512]
    pt = lambda sl: sc1[:, s["pt_off"] + sl * 512: s["pt_off"] + (sl + 1) * 512]
    rr = s["sc1_f32"][:, s["rr_off"] // 2: s["rr_off"] // 2 + 512]
    for h in range(c.H):
        for r in range(2):
            ksl = (2 * h + r) % 3
            p.op("sp", lambda e, h=h, r=r, ksl=ksl: e.dma_start(
                out=kh(ksl), in_=d["kv_out"][256 * r:256 * r + 128, h * c.TOK:(h + 1) * c.TOK]),
                reads=[("kv_out",)], writes=[("at", "kh", ksl)],
                extra_deps=fence if h == 0 else (), tag=f"a{L}kh{h}_{r}")
            vsl = r
            p.op("sp", lambda e, h=h, r=r, vsl=vsl: e.dma_start(
                out=vh(vsl), in_=d["kv_out"][256 * r + 128:256 * r + 256,
                                             h * c.NJ * 128:(h + 1) * c.NJ * 128]),
                reads=[("kv_out",)], writes=[("at", "vh", vsl)],
                extra_deps=fence if h == 0 else (), tag=f"a{L}vh{h}_{r}")
        for qc in range(NQC):
            qsl = qc % 2
            p.op("sp", lambda e, h=h, qc=qc, qsl=qsl: e.dma_start(
                out=qh(qsl), in_=d["q_dram"][:, h * c.TOK + qc * 512: h * c.TOK + (qc + 1) * 512]),
                reads=[("q_dram",)], writes=[("at", "qh", qsl)],
                extra_deps=fence if h == 0 else (), tag=f"a{L}q{h}_{qc}")
            n_kb = 8 * qc + 8 if causal else c.NBLK
            groups = [list(range(g, min(g + 2, n_kb))) for g in range(0, n_kb, 2)]

            def bank_of(kb):
                return ((kb // 2) % 2) * 2 + (kb % 2)

            def emit_scores(gi):
                for kb in groups[gi]:
                    bank = bank_of(kb)
                    jlo = max(4 * qc, kb // 2) if causal else 4 * qc
                    off = (jlo - 4 * qc) * 128
                    r_, jb = kb % 2, kb // 2
                    ksl = (2 * h + r_) % 3
                    p.op("pe", lambda e, bank=bank, off=off, jb=jb, ksl=ksl, qsl=qsl: e.matmul(
                        x.psc[:, bank * 512 + off: (bank + 1) * 512],
                        kh(ksl)[:, jb * 128:(jb + 1) * 128],
                        qh(qsl)[:, off:512], start=True, stop=True),
                        reads=[("at", "kh", ksl), ("at", "qh", qsl)],
                        writes=[("pb", bank)], tag=f"a{L}s{h}_{qc}_{kb}")
                    if causal and kb // 2 >= 4 * qc:
                        mcol = (kb % 2) * 128
                        p.op("dve", lambda e, bank=bank, off=off, mcol=mcol: e.tensor_tensor(
                            out=x.psc[:, bank * 512 + off: bank * 512 + off + 128],
                            in0=x.psc[:, bank * 512 + off: bank * 512 + off + 128],
                            in1=s["trif"][:, mcol:mcol + 128], op=ALU.add),
                            reads=[("pb", bank), ("sb", "tri")], writes=[("pb", bank)],
                            tag=f"a{L}m{h}_{qc}_{kb}")

            emit_scores(0)
            for gi in range(len(groups)):
                if gi + 1 < len(groups):
                    emit_scores(gi + 1)
                g = groups[gi]
                lo = bank_of(g[0]) * 512
                hi = (bank_of(g[-1]) + 1) * 512
                bcol = 2 if L == 0 else 3
                p.op("act", lambda e, lo=lo, hi=hi, bcol=bcol: e.activation(
                    out=sc1[:, s["pt_off"] + lo: s["pt_off"] + hi],
                    in_=x.psc[:, lo:hi], func=AF.Exp, bias=s["cst"][:, bcol:bcol + 1],
                    scale=scale),
                    reads=[("pb", bank_of(kb)) for kb in g],
                    writes=[("at", "pt", bank_of(kb)) for kb in g],
                    tag=f"a{L}x{h}_{qc}_{gi}")
                for kb in g:
                    bank = bank_of(kb)
                    jlo = max(4 * qc, kb // 2) if causal else 4 * qc
                    off = (jlo - 4 * qc) * 128
                    r_, jb = kb % 2, kb // 2
                    vsl = r_
                    p.op("pe", lambda e, kb=kb, bank=bank, off=off, jb=jb, vsl=vsl: e.matmul(
                        x.pb[4][:, off:512], vh(vsl)[:, jb * 128:(jb + 1) * 128],
                        pt(bank)[:, off:512], start=(kb == 0), stop=(kb == n_kb - 1)),
                        reads=[("at", "vh", vsl), ("at", "pt", bank)],
                        writes=[("pb", 4)], tag=f"a{L}v{h}_{qc}_{kb}")
                    p.op("pe", lambda e, kb=kb, bank=bank, off=off: e.matmul(
                        x.pb[5][0:1, off:512], s["onesb"][:, 0:1],
                        pt(bank)[:, off:512], start=(kb == 0), stop=(kb == n_kb - 1)),
                        reads=[("at", "pt", bank), ("sb", "ones_bf")],
                        writes=[("pb", 5)], tag=f"a{L}u{h}_{qc}_{kb}")
            p.op("dve", lambda e: e.reciprocal(out=rr[0:1, 0:512], in_=x.pb[5][0:1, 0:512]),
                 reads=[("pb", 5)], writes=[("at", "rr")], tag=f"a{L}r{h}_{qc}")
            if o_scale != 1.0:
                p.op("dve", lambda e: e.tensor_scalar(
                    out=rr[0:1, 0:512], in0=rr[0:1, 0:512], scalar1=o_scale, scalar2=None,
                    op0=ALU.mult),
                    reads=[("at", "rr")], writes=[("at", "rr")], tag=f"a{L}rs{h}_{qc}")
            p.op("pe", lambda e: e.matmul(
                x.pb[6][:, 0:512], s["onesf"][:, :], rr[0:1, 0:512], start=True, stop=True),
                reads=[("at", "rr"), ("sb", "ones_f32")], writes=[("pb", 6)],
                tag=f"a{L}rb{h}_{qc}")
            p.op("act", lambda e: e.copy(out=rr, in_=x.pb[6][:, 0:512]),
                 reads=[("pb", 6)], writes=[("at", "rr")], tag=f"a{L}rc{h}_{qc}")
            p.op("dve", lambda e, h=h, qc=qc: e.scalar_tensor_tensor(
                out=s["ar1"][:, h * c.TOK + qc * 512: h * c.TOK + (qc + 1) * 512],
                in0=x.pb[4][:, 0:512], scalar=1.0, in1=rr, op0=ALU.mult, op1=ALU.mult),
                reads=[("pb", 4), ("at", "rr")], writes=[("a1", "o", h, qc)],
                tag=f"a{L}o{h}_{qc}")


def wbk(lo, n):
    return [("wb", i) for i in range(lo // 512, (lo + n) // 512)]


def fxk(lo, n):
    return [("fx", i) for i in range(lo // 512, (lo + n) // 512)]


def ph_proj(x, L, x_src):
    c, p, d, s = x.cfg, x.prog, x.d, x.s
    sc1 = s["sc1"]
    wsrc = d["w_proj"] if L == 0 else d["w_out"]
    if getattr(x.cfg, "debug_o", None) == L:
        oreads = [("a1", "o", hh, qq) for hh in range(c.H)
                  for qq in range(max(1, c.NJ // 4))]
        p.op("sp", lambda e: e.dma_start(out=d["dbg_o"][:, 0:c.ECH * c.TOK],
                                         in_=s["ar1"][:, 0:c.ECH * c.TOK]),
             reads=oreads, writes=[("dbg_o",)], tag=f"dbgo{L}")
    fence = (p.snapshot(("at",)) | p.snapshot(("fx",)) | p.snapshot(("gbrep",))
             | p.snapshot(("xnt",)) | p.snapshot(("lgf",)) | p.snapshot(("rp",)))
    xn_f = s["sc1_f32"][:, s["kh_off"] // 2: s["kh_off"] // 2 + 2048]
    for oc in range(4):
        for fq in range(4):
            p.op("poolq", lambda e, oc=oc, fq=fq: e.dma_start(
                out=s["wbuf"][:, fq * 2048:(fq + 1) * 2048].rearrange("p (c n) -> p c n", n=512),
                in_=wsrc[fq * 512:(fq + 1) * 512, oc * 512:(oc + 1) * 512]
                .rearrange("(c p) n -> p c n", p=128)),
                reads=[(wsrc.tensor.name,)], writes=wbk(fq * 2048, 2048), tag=f"p{L}w{oc}{fq}")
        if L == 0:
            p.op("poolq", lambda e, oc=oc: e.dma_start(
                out=s["brow"][0:1, 0:512], in_=d["b_proj"][:, oc * 512:(oc + 1) * 512]),
                reads=[("b_proj",)], writes=[("brow", 0)], tag=f"p{L}b{oc}")
        for tb in range(c.NJ):
            bank = (oc * c.NJ + tb) % 4
            for hc in range(c.ECH):
                p.op("pe", lambda e, oc=oc, tb=tb, hc=hc, bank=bank: e.matmul(
                    x.pb[bank][:, 0:512],
                    s["ar1"][:, hc * c.TOK + tb * 128: hc * c.TOK + (tb + 1) * 128],
                    s["wbuf"][:, hc * 512:(hc + 1) * 512],
                    start=(hc == 0), stop=(L == 1 and hc == c.ECH - 1)),
                    reads=wbk(hc * 512, 512) + [("a1", "o", hc, tb // 4)],
                    writes=[("pb", bank)], tag=f"p{L}m{oc}_{tb}_{hc}")
            if L == 0:
                p.op("pe", lambda e, bank=bank: e.matmul(
                    x.pb[bank][:, 0:512], s["onesb"][0:1, 0:128], s["brow"][0:1, 0:512],
                    start=False, stop=True),
                    reads=[("brow", 0), ("sb", "ones_bf")], writes=[("pb", bank)],
                    tag=f"p{L}bm{oc}_{tb}")
            fsl = (oc * c.NJ + tb) % 4
            fx = s["fx"][:, fsl * 512:(fsl + 1) * 512]
            p.op("sp", lambda e, oc=oc, tb=tb, fx=fx: e.dma_start(
                out=fx, in_=x_src[tb * 128:(tb + 1) * 128, oc * 512:(oc + 1) * 512]),
                reads=[(x_src.tensor.name,)], writes=fxk(fsl * 512, 512),
                extra_deps=fence, tag=f"p{L}x{oc}_{tb}")
            if L == 1:
                p.op("sp", lambda e, oc=oc, tb=tb: e.dma_start(
                    out=xn_f[:, oc * 512:(oc + 1) * 512],
                    in_=d["xn_tm_d"][tb * 128:(tb + 1) * 128, oc * 512:(oc + 1) * 512]),
                    reads=[("xn_tm_d",)], writes=[("pxn", oc)],
                    extra_deps=fence, tag=f"p{L}n{oc}_{tb}")
            p.op("dve", lambda e, fx=fx, bank=bank: e.scalar_tensor_tensor(
                out=fx, in0=x.pb[bank][:, 0:512], scalar=1.0, in1=fx,
                op0=ALU.mult, op1=ALU.add),
                reads=[("pb", bank)] + fxk(fsl * 512, 512), writes=fxk(fsl * 512, 512),
                tag=f"p{L}a{oc}_{tb}")
            if L == 1:
                p.op("dve", lambda e, fx=fx, oc=oc: e.tensor_tensor(
                    out=fx, in0=fx, in1=xn_f[:, oc * 512:(oc + 1) * 512], op=ALU.add),
                    reads=fxk(fsl * 512, 512) + [("pxn", oc)],
                    writes=fxk(fsl * 512, 512), tag=f"p{L}an{oc}_{tb}")
            p.op("sp", lambda e, oc=oc, tb=tb, fx=fx: e.dma_start(
                out=d["xp"][tb * 128:(tb + 1) * 128, oc * 512:(oc + 1) * 512], in_=fx),
                reads=fxk(fsl * 512, 512), writes=[("xp",)], tag=f"p{L}s{oc}_{tb}")


def ph_ffn0(x, x_dst):
    c, p, d, s = x.cfg, x.prog, x.d, x.s
    sc1 = s["sc1"]
    inv_d = 1.0 / c.D
    fence = (p.snapshot(("at",)) | p.snapshot(("dq",)) | p.snapshot(("xnt",))
             | p.snapshot(("a1",)) | p.snapshot(("fx",)) | p.snapshot(("rp",))
             | p.snapshot(("brow",)) | p.snapshot(("lgf",)))
    stg = lambda sl: s["fx"][:, 3072 + sl * 512: 3072 + (sl + 1) * 512]
    yv = s["ar1_f32"]
    for tc in range(c.NTC):
        first = (tc == 0)
        for ib in range(c.IB):
            for half, base in ((0, 0), (1, 4096)):
                gIb = ib + half * c.IB
                r, lb = gIb // 11, gIb % 11
                slot = base + (ib % 2) * 2048
                p.op("sp", lambda e, r=r, lb=lb, slot=slot: e.dma_start(
                    out=s["wbuf"][:, slot:slot + 2048],
                    in_=d["wgu"][r * 128:(r + 1) * 128, lb * 2048:(lb + 1) * 2048]),
                    reads=[("wgu",)], writes=wbk(slot, 2048),
                    extra_deps=fence if first and ib < 2 else (), tag=f"f{tc}w{gIb}")
            bg, bu = ib % 2, 2 + ib % 2
            for fc in range(c.ECH):
                for bank, base in ((bg, 0), (bu, 4096)):
                    slot = base + (ib % 2) * 2048
                    p.op("pe", lambda e, fc=fc, tc=tc, bank=bank, slot=slot: e.matmul(
                        x.pb[bank][:, 0:512],
                        s["wbuf"][:, slot + fc * 128: slot + (fc + 1) * 128],
                        s["xnfm"][:, fc * c.TOK + tc * 512: fc * c.TOK + (tc + 1) * 512],
                        start=(fc == 0), stop=(fc == c.ECH - 1)),
                        reads=[("wb", (slot + fc * 128) // 512)]
                        + [("xnfm", fc, 4 * tc + i) for i in range(4)],
                        writes=[("pb", bank)], tag=f"f{tc}m{ib}_{fc}_{bank}")
            ssl = ib % 2
            p.op("act", lambda e, bg=bg, ssl=ssl: e.activation(
                out=stg(ssl), in_=x.pb[bg][:, 0:512], func=AF.Silu),
                reads=[("pb", bg)], writes=fxk(3072 + ssl * 512, 512),
                extra_deps=fence if first and ib == 0 else (), tag=f"f{tc}sl{ib}")
            p.op("dve", lambda e, bu=bu, ssl=ssl, ib=ib, tc=tc: e.scalar_tensor_tensor(
                out=s["hsb"][:, ib * c.TC + (tc % 1) * 0: ib * c.TC + 512], in0=x.pb[bu][:, 0:512],
                scalar=1.0, in1=stg(ssl), op0=ALU.mult, op1=ALU.mult),
                reads=[("pb", bu)] + fxk(3072 + ssl * 512, 512), writes=[("h", ib)],
                extra_deps=fence if first and ib == 0 else (), tag=f"f{tc}h{ib}")
        for oc in range(4):
            for ib in range(c.IB):
                r, lb = ib // 6, ib % 6
                wsl = (ib % 4) * 512
                p.op("sp", lambda e, r=r, lb=lb, oc=oc, wsl=wsl: e.dma_start(
                    out=s["wbuf"][:, wsl:wsl + 512],
                    in_=d["wdn"][r * 128:(r + 1) * 128,
                                 lb * 2048 + oc * 512: lb * 2048 + (oc + 1) * 512]),
                    reads=[("wdn",)], writes=wbk(wsl, 512), tag=f"f{tc}d{oc}_{ib}")
                for tb4 in range(4):
                    p.op("pe", lambda e, ib=ib, tb4=tb4, wsl=wsl: e.matmul(
                        x.pb[4 + tb4][:, 0:512],
                        s["hsb"][:, ib * c.TC + tb4 * 128: ib * c.TC + (tb4 + 1) * 128],
                        s["wbuf"][:, wsl:wsl + 512],
                        start=(ib == 0), stop=(ib == c.IB - 1)),
                        reads=wbk(wsl, 512) + [("h", ib)], writes=[("pb", 4 + tb4)],
                        tag=f"f{tc}dm{oc}_{ib}_{tb4}")
            for tb4 in range(4):
                p.op("act", lambda e, oc=oc, tb4=tb4: e.copy(
                    out=yv[:, tb4 * 2048 + oc * 512: tb4 * 2048 + (oc + 1) * 512],
                    in_=x.pb[4 + tb4][:, 0:512]),
                    reads=[("pb", 4 + tb4)], writes=[("a1", "y", tb4, oc)],
                    extra_deps=fence if first and oc == 0 else (), tag=f"f{tc}ye{oc}_{tb4}")
        for tb4 in range(4):
            tb = tc * 4 + tb4
            yt = yv[:, tb4 * 2048:(tb4 + 1) * 2048]
            ykeys = [("a1", "y", tb4, oc) for oc in range(4)]
            p.op("act", lambda e, yt=yt: e.activation(
                out=s["hsb"][:, 0:2048], in_=yt, func=AF.Square, accum_out=s["stat"][:, 8:9]),
                reads=ykeys, writes=[("st", 8)] + [("h", i) for i in range(4)],
                tag=f"f{tc}sq{tb4}")
            p.op("act", lambda e: e.activation(out=s["stat"][:, 9:10], in_=s["stat"][:, 8:9],
                                               func=AF.Sqrt, bias=s["cst"][:, 1:2], scale=inv_d),
                 reads=[("st", 8), ("sb", "cvec")], writes=[("st", 9)], tag=f"f{tc}rm{tb4}")
            p.op("dve", lambda e: e.reciprocal(out=s["stat"][:, 10:11], in_=s["stat"][:, 9:10]),
                 reads=[("st", 9)], writes=[("st", 10)], tag=f"f{tc}rc{tb4}")
            fsl = 0
            fx = s["fx"][:, fsl * 2048:(fsl + 1) * 2048]
            p.op("sp", lambda e, tb=tb, fx=fx: e.dma_start(
                out=fx, in_=d["xp"][tb * 128:(tb + 1) * 128, :]),
                reads=[("xp",)], writes=fxk(fsl * 2048, 2048), tag=f"f{tc}xp{tb4}")
            p.op("dve", lambda e, yt=yt, fx=fx: e.scalar_tensor_tensor(
                out=fx, in0=yt, scalar=s["stat"][:, 10:11], in1=fx,
                op0=ALU.mult, op1=ALU.add),
                reads=ykeys + [("st", 10)] + fxk(fsl * 2048, 2048),
                writes=fxk(fsl * 2048, 2048), tag=f"f{tc}fin{tb4}")
            p.op("sp", lambda e, tb=tb, fx=fx: e.dma_start(
                out=x_dst[tb * 128:(tb + 1) * 128, :], in_=fx),
                reads=fxk(fsl * 2048, 2048), writes=[(x_dst.tensor.name,)],
                tag=f"f{tc}st{tb4}")


def ph_qkv1(x):
    c, p, d, s = x.cfg, x.prog, x.d, x.s
    sc1 = s["sc1"]
    rsc = s["rsc"]
    for ccb in range(12):
        for fq in range(4):
            p.op("poolq", lambda e, ccb=ccb, fq=fq: e.dma_start(
                out=s["wbuf"][:, fq * 2048:(fq + 1) * 2048].rearrange("p (c n) -> p c n", n=512),
                in_=d["w_in"][fq * 512:(fq + 1) * 512, ccb * 512:(ccb + 1) * 512]
                .rearrange("(c p) n -> p c n", p=128)),
                reads=[("w_in",)], writes=wbk(fq * 2048, 2048), tag=f"q1w{ccb}_{fq}")
        for tb in range(c.NJ):
            bank = tb % 4
            for fc in range(c.ECH):
                p.op("pe", lambda e, tb=tb, fc=fc, bank=bank: e.matmul(
                    x.pb[bank][:, 0:512],
                    s["xnfm"][:, fc * c.TOK + tb * 128: fc * c.TOK + (tb + 1) * 128],
                    s["wbuf"][:, fc * 512:(fc + 1) * 512],
                    start=(fc == 0), stop=(fc == c.ECH - 1)),
                    reads=wbk(fc * 512, 512) + [("xnfm", fc, tb)],
                    writes=[("pb", bank)], tag=f"q1m{ccb}_{tb}_{fc}")
            if ccb >= 8:
                cc = ccb - 8
                p.op("act", lambda e, cc=cc, tb=tb, bank=bank: e.copy(
                    out=s["ar1"][:, 0:c.ECH * c.TOK]
                    .rearrange("p (h j n) -> p h j n", h=c.H, j=c.NJ)
                    [:, 4 * cc:4 * cc + 4, tb:tb + 1, :],
                    in_=x.pb[bank][:, 0:512].rearrange("p (j o n) -> p j o n", j=4, o=1)),
                    reads=[("pb", bank)], writes=[("a1", "v", cc, tb)], tag=f"q1v{cc}_{tb}")
                continue
            p.op("sp", lambda e, tb=tb: e.dma_start(
                out=s["cost"][:, 0:32], in_=d["cos_own"][tb * 128:(tb + 1) * 128, :]),
                reads=[("cos_own",)], writes=[("rc", "c")], tag=f"q1c{ccb}_{tb}")
            p.op("sp", lambda e, tb=tb: e.dma_start(
                out=s["sint"][:, 0:32], in_=d["sin_own"][tb * 128:(tb + 1) * 128, :]),
                reads=[("sin_own",)], writes=[("rc", "s")], tag=f"q1sn{ccb}_{tb}")
            qt = s["fx"][:, 0:512]
            p.op("act", lambda e, bank=bank, qt=qt: e.copy(out=qt, in_=x.pb[bank][:, 0:512]),
                 reads=[("pb", bank)], writes=fxk(0, 512), tag=f"q1e{ccb}_{tb}")
            ev = qt.rearrange("p (h n) -> p h n", h=4)[:, :, 0:64:2]
            od = qt.rearrange("p (h n) -> p h n", h=4)[:, :, 1:64:2]
            cosb = s["cost"][:, 0:32].unsqueeze(1).broadcast_to([128, 4, 32])
            sinb = s["sint"][:, 0:32].unsqueeze(1).broadcast_to([128, 4, 32])
            tA = rsc[:, 0:128].rearrange("p (h n) -> p h n", h=4)
            tB = rsc[:, 128:256].rearrange("p (h n) -> p h n", h=4)
            tC = rsc[:, 256:384].rearrange("p (h n) -> p h n", h=4)
            tD = rsc[:, 384:512].rearrange("p (h n) -> p h n", h=4)
            p.op("dve", lambda e, ev=ev, sinb=sinb, tA=tA: e.tensor_tensor(out=tA, in0=ev, in1=sinb, op=ALU.mult),
                 reads=fxk(0, 512) + [("rc", "s")], writes=[("fx", 1)], tag=f"q1r1_{ccb}_{tb}")
            p.op("dve", lambda e, ev=ev, cosb=cosb, tB=tB: e.tensor_tensor(out=tB, in0=ev, in1=cosb, op=ALU.mult),
                 reads=fxk(0, 512) + [("rc", "c")], writes=[("fx", 1)], tag=f"q1r2_{ccb}_{tb}")
            p.op("dve", lambda e, od=od, sinb=sinb, tC=tC: e.tensor_tensor(out=tC, in0=od, in1=sinb, op=ALU.mult),
                 reads=fxk(0, 512) + [("rc", "s")], writes=[("fx", 1)], tag=f"q1r3_{ccb}_{tb}")
            p.op("dve", lambda e, od=od, cosb=cosb, tD=tD: e.tensor_tensor(out=tD, in0=od, in1=cosb, op=ALU.mult),
                 reads=fxk(0, 512) + [("rc", "c")], writes=[("fx", 1)], tag=f"q1r4_{ccb}_{tb}")
            p.op("dve", lambda e, ev=ev, tB=tB, tC=tC: e.tensor_tensor(out=ev, in0=tB, in1=tC, op=ALU.subtract),
                 reads=[("fx", 1)], writes=fxk(0, 512), tag=f"q1r5_{ccb}_{tb}")
            p.op("dve", lambda e, od=od, tA=tA, tD=tD: e.tensor_tensor(out=od, in0=tD, in1=tA, op=ALU.add),
                 reads=[("fx", 1)], writes=fxk(0, 512), tag=f"q1r6_{ccb}_{tb}")
            stg = sc1[:, s["stg_off"]:s["stg_off"] + 512]
            p.op("dve", lambda e, qt=qt, stg=stg: e.tensor_copy(out=stg, in_=qt),
                 reads=fxk(0, 512), writes=[("at", "stg", 0)], tag=f"q1cv{ccb}_{tb}")
            for j in range(4):
                hh = (ccb % 4) * 4 + j
                bank2 = 4 + j % 2
                p.op("pe", lambda e, j=j, bank2=bank2, stg=stg: e.transpose(
                    x.pb_bf[bank2][:, 0:128], stg[:, j * 128:(j + 1) * 128], s["identb"][:, :]),
                    reads=[("at", "stg", 0), ("sb", "ident_bf")], writes=[("pb", bank2)],
                    tag=f"q1t{ccb}_{tb}_{j}")
                st2 = sc1[:, s["stg_off"] + 512 + (j % 2) * 128: s["stg_off"] + 512 + (j % 2 + 1) * 128]
                p.op("act", lambda e, bank2=bank2, st2=st2: e.copy(
                    out=st2, in_=x.pb_bf[bank2][:, 0:128]),
                    reads=[("pb", bank2)], writes=[("at", "stg", 1)], tag=f"q1te{ccb}_{tb}_{j}")
                if ccb < 4:
                    dst = d["q_dram"][:, hh * c.TOK + tb * 128: hh * c.TOK + (tb + 1) * 128]
                    wkey = ("q_dram",)
                else:
                    dst = d["kv_in"][0:128, hh * c.TOK + tb * 128: hh * c.TOK + (tb + 1) * 128]
                    wkey = ("kv_in",)
                p.op("sp", lambda e, dst=dst, st2=st2: e.dma_start(out=dst, in_=st2),
                     reads=[("at", "stg", 1)], writes=[wkey], tag=f"q1st{ccb}_{tb}_{j}")


def ph_xn_reload(x):
    c, p, d, s = x.cfg, x.prog, x.d, x.s
    for tb in range(c.NJ):
        sl = tb % 2
        xt = s["fx"][:, sl * c.D:(sl + 1) * c.D]
        p.op("sp", lambda e, tb=tb, xt=xt: e.dma_start(
            out=xt, in_=d["xn_tm_d"][tb * 128:(tb + 1) * 128, :]),
            reads=[("xn_tm_d",)], writes=fxk(sl * c.D, 2048), tag=f"xr{tb}")
        xnt = s["sc1"][:, s["xnt_off"]:s["xnt_off"] + 2048]
        p.op("dve", lambda e, xt=xt, xnt=xnt: e.tensor_copy(out=xnt, in_=xt),
             reads=fxk(sl * c.D, 2048), writes=[("xnt",)], tag=f"xrc{tb}")
        for fc in range(c.ECH):
            bank = 4 + (fc % 2)
            p.op("pe", lambda e, fc=fc, bank=bank, xnt=xnt: e.transpose(
                x.pb_bf[bank][:, 0:128], xnt[:, fc * 128:(fc + 1) * 128], s["identb"][:, :]),
                reads=[("xnt",), ("sb", "ident_bf")], writes=[("pb", bank)], tag=f"xrt{tb}_{fc}")
            p.op("act", lambda e, fc=fc, tb=tb, bank=bank: e.copy(
                out=s["xnfm"][:, fc * c.TOK + tb * 128: fc * c.TOK + (tb + 1) * 128],
                in_=x.pb_bf[bank][:, 0:128]),
                reads=[("pb", bank)], writes=[("xnfm", fc, tb)], tag=f"xre{tb}_{fc}")


def ph_gate(x):
    c, p, d, s = x.cfg, x.prog, x.d, x.s
    gfence = p.snapshot(("fx",)) | p.snapshot(("rp",))
    for tc in range(c.NTC):
        bank = tc % 2
        for fc in range(c.ECH):
            p.op("pe", lambda e, tc=tc, fc=fc, bank=bank: e.matmul(
                x.pb[bank][0:8, 0:512], s["gwlt"][:, fc * 8:(fc + 1) * 8],
                s["xnfm"][:, fc * c.TOK + tc * 512: fc * c.TOK + (tc + 1) * 512],
                start=(fc == 0), stop=(fc == c.ECH - 1)),
                reads=[("sb", "gwlt"), ("xnfm", fc, 4 * tc), ("xnfm", fc, 4 * tc + 1),
                       ("xnfm", fc, 4 * tc + 2), ("xnfm", fc, 4 * tc + 3)],
                writes=[("pb", bank)], tag=f"g{tc}m{fc}")
        p.op("act", lambda e, bank=bank: e.activation(
            out=s["lgf"][0:8, 0:512], in_=x.pb[bank][0:8, 0:512], func=AF.Identity,
            bias=s["gateb"][0:8, :], scale=1.0),
            reads=[("pb", bank), ("sb", "gate_b")], writes=[("fx", 2)],
            extra_deps=gfence, tag=f"g{tc}e")
        for j in range(4):
            tb = tc * 4 + j
            bank2 = 4 + j % 2
            p.op("pe", lambda e, j=j, bank2=bank2: e.transpose(
                x.pb[bank2][:, 0:8], s["lgf"][0:8, j * 128:(j + 1) * 128], s["identf"][:, :]),
                reads=[("fx", 2), ("sb", "ident_f32")], writes=[("pb", bank2)],
                tag=f"g{tc}t{j}")
            p.op("act", lambda e, bank2=bank2: e.copy(out=s["lgt"][:, 0:8], in_=x.pb[bank2][:, 0:8]),
                 reads=[("pb", bank2)], writes=[("lgt",)], tag=f"g{tc}te{j}")
            p.op("dve", lambda e: e.reduce_max(out=s["stat"][:, 11:12], in_=s["lgt"][:, 0:8], axis=AX.X),
                 reads=[("lgt",)], writes=[("st", 11)], tag=f"g{tc}x1{j}")
            p.op("dve", lambda e: e.tensor_scalar(
                out=s["lgs"][:, 0:8], in0=s["lgt"][:, 0:8], scalar1=s["stat"][:, 11:12],
                scalar2=-1e30, op0=ALU.is_ge, op1=ALU.mult),
                reads=[("lgt",), ("st", 11)], writes=[("lgs",)], tag=f"g{tc}ms{j}")
            p.op("dve", lambda e: e.tensor_tensor(out=s["lgs"][:, 0:8], in0=s["lgs"][:, 0:8],
                                                  in1=s["lgt"][:, 0:8], op=ALU.add),
                 reads=[("lgs",), ("lgt",)], writes=[("lgs",)], tag=f"g{tc}ad{j}")
            p.op("dve", lambda e: e.reduce_max(out=s["stat"][:, 12:13], in_=s["lgs"][:, 0:8], axis=AX.X),
                 reads=[("lgs",)], writes=[("st", 12)], tag=f"g{tc}x2{j}")
            p.op("dve", lambda e, tb=tb: e.tensor_scalar(
                out=s["msk"][:, tb * 8:(tb + 1) * 8], in0=s["lgt"][:, 0:8],
                scalar1=s["stat"][:, 12:13], scalar2=0.5, op0=ALU.is_ge, op1=ALU.mult),
                reads=[("lgt",), ("st", 12)], writes=[("msk", tb)], tag=f"g{tc}mk{j}")


def ph_moe(x, x_dst):
    c, p, d, s = x.cfg, x.prog, x.d, x.s
    sc1 = s["sc1"]
    fence = p.snapshot(("a1",)) | p.snapshot(("at",)) | p.snapshot(("h",))
    stg = lambda sl: sc1[:, s["stg_off"] + sl * 512: s["stg_off"] + (sl + 1) * 512]
    ym = s["ar1"]
    for e_i in range(c.NE):
        for oc in range(4):
            bsl = oc % 2
            p.op("poolq", lambda e, e_i=e_i, oc=oc, bsl=bsl: e.dma_start(
                out=s["brow"][0:1, bsl * 512:(bsl + 1) * 512],
                in_=d["exp_b"][e_i:e_i + 1, oc * 512:(oc + 1) * 512]),
                reads=[("exp_b",)], writes=[("brow", bsl)], tag=f"e{e_i}b{oc}")
            for fq in range(4):
                p.op("sp", lambda e, e_i=e_i, oc=oc, fq=fq: e.dma_start(
                    out=s["wbuf"][:, fq * 2048:(fq + 1) * 2048].rearrange("p (c n) -> p c n", n=512),
                    in_=d["expw"][e_i * c.D + fq * 512:e_i * c.D + (fq + 1) * 512,
                                  oc * 512:(oc + 1) * 512].rearrange("(c p) n -> p c n", p=128)),
                    reads=[("expw",)], writes=wbk(fq * 2048, 2048), tag=f"e{e_i}w{oc}{fq}")
            for tb in range(c.NJ):
                bank = tb % 4
                for fc in range(c.ECH):
                    p.op("pe", lambda e, tb=tb, fc=fc, bank=bank: e.matmul(
                        x.pb[bank][:, 0:512],
                        s["xnfm"][:, fc * c.TOK + tb * 128: fc * c.TOK + (tb + 1) * 128],
                        s["wbuf"][:, fc * 512:(fc + 1) * 512],
                        start=(fc == 0), stop=False),
                        reads=wbk(fc * 512, 512) + [("xnfm", fc, tb)],
                        writes=[("pb", bank)], tag=f"e{e_i}m{oc}_{tb}_{fc}")
                p.op("pe", lambda e, bsl=bsl, bank=bank: e.matmul(
                    x.pb[bank][:, 0:512], s["onesb"][0:1, 0:128],
                    s["brow"][0:1, bsl * 512:(bsl + 1) * 512], start=False, stop=True),
                    reads=[("brow", bsl), ("sb", "ones_bf")], writes=[("pb", bank)],
                    tag=f"e{e_i}bm{oc}_{tb}")
                ssl = tb % 2
                p.op("act", lambda e, bank=bank, ssl=ssl: e.activation(
                    out=stg(ssl), in_=x.pb[bank][:, 0:512], func=AF.Relu),
                    reads=[("pb", bank)], writes=[("at", "stg", ssl)],
                    extra_deps=fence if e_i == 0 and oc == 0 and tb == 0 else (),
                    tag=f"e{e_i}r{oc}_{tb}")
                ydst = ym[:, tb * 2048 + oc * 512: tb * 2048 + (oc + 1) * 512]
                mcol = s["msk"][:, tb * 8 + e_i: tb * 8 + e_i + 1]
                if e_i == 0:
                    p.op("dve", lambda e, ydst=ydst, mcol=mcol, ssl=ssl: e.tensor_scalar(
                        out=ydst, in0=stg(ssl), scalar1=mcol, scalar2=None, op0=ALU.mult),
                        reads=[("at", "stg", ssl), ("msk", tb)], writes=[("a1", "ym", tb, oc)],
                        extra_deps=fence, tag=f"e{e_i}y{oc}_{tb}")
                else:
                    p.op("dve", lambda e, ydst=ydst, mcol=mcol, ssl=ssl: e.scalar_tensor_tensor(
                        out=ydst, in0=stg(ssl), scalar=mcol, in1=ydst,
                        op0=ALU.mult, op1=ALU.add),
                        reads=[("at", "stg", ssl), ("msk", tb), ("a1", "ym", tb, oc)],
                        writes=[("a1", "ym", tb, oc)], tag=f"e{e_i}y{oc}_{tb}")
    for tb in range(c.NJ):
        fsl = tb % 2
        fx = s["fx"][:, fsl * 2048:(fsl + 1) * 2048]
        p.op("sp", lambda e, tb=tb, fx=fx: e.dma_start(
            out=fx, in_=d["xp"][tb * 128:(tb + 1) * 128, :]),
            reads=[("xp",)], writes=fxk(fsl * 2048, 2048), tag=f"my{tb}x")
        p.op("dve", lambda e, tb=tb, fx=fx: e.tensor_tensor(
            out=fx, in0=fx, in1=ym[:, tb * 2048:(tb + 1) * 2048], op=ALU.add),
            reads=fxk(fsl * 2048, 2048) + [("a1", "ym", tb, oc) for oc in range(4)],
            writes=fxk(fsl * 2048, 2048), tag=f"my{tb}a")
        p.op("sp", lambda e, tb=tb, fx=fx: e.dma_start(
            out=x_dst[tb * 128:(tb + 1) * 128, :], in_=fx),
            reads=fxk(fsl * 2048, 2048), writes=[(x_dst.tensor.name,)], tag=f"my{tb}s")


def build_phases(x, layers):
    c, d = x.cfg, x.d
    ph_consts(x)
    for L in layers:
        src = d["x0"] if L == layers[0] else d["x_mid"]
        dst = d["y"] if L == layers[-1] else d["x_mid"]
        if L == 0:
            ph_ln(x, 0, src)
            ph_dequant(x)
            ph_qkv0(x)
            ph_ag_kv(x, 0)
            ph_attn(x, 0)
            ph_proj(x, 0, src)
            ph_ffn0(x, dst)
        else:
            ph_gamma_beta(x)
            ph_ln(x, 1, src)
            ph_experts_convert(x)
            ph_qkv1(x)
            ph_ag_kv(x, 1)
            ph_xn_reload(x)
            ph_gate(x)
            ph_attn(x, 1)
            ph_proj(x, 1, src)
            ph_moe(x, dst)


def build_program(cfg, layers=(0, 1)):
    nc = bass.Bass("TRN2", target_bir_lowering=False)
    es = contextlib.ExitStack()
    with es:
        x = declare(cfg, nc, es)
        build_phases(x, layers)
        x.prog.finalize(nc, es)
    return nc



_CFG = Cfg()


def kernel(**inputs):
    cfg = _CFG
    in_maps = host_shard(cfg, inputs)
    nc = build_program(cfg, layers=(0, 1))
    res = run_bass_kernel_spmd(nc, in_maps, core_ids=list(range(cfg.n_cores)))
    outs = [res.results[i]["y"] for i in range(cfg.n_cores)]
    return host_assemble(cfg, outs)
